# version 02
# baseline (speedup 1.0000x reference)
import contextlib
import numpy as np
import ml_dtypes
import concourse.bass as bass
import concourse.mybir as mybir
from concourse.bass_utils import run_bass_kernel_spmd

F32 = mybir.dt.float32
BF16 = mybir.dt.bfloat16
I32 = mybir.dt.int32
OP = mybir.AluOpType
AX = mybir.AxisListType
AF = mybir.ActivationFunctionType
BF = ml_dtypes.bfloat16

D = 4096
H = 32
DH = 128
NT = 8
SCALE = float(DH) ** -0.5
LN_EPS = 1e-5
CLIP = 8.0

def alibi_slopes():
    m = np.arange(1, 33, dtype=np.float64) * (8.0 / 32.0)
    return 1.0 / (2.0 ** m)

SLOPES = alibi_slopes()



def serializer(eng, sem):
    state = {"n": 0}
    def ser(inst):
        state["n"] += 1
        inst.then_inc(sem, 1)
        eng.wait_ge(sem, state["n"])
        return inst
    return ser


def emit_ln_lnT(nc, sems, tag, x_dram_tiles, lnT, ident, epst=None):
    with contextlib.ExitStack() as ctx:
        xt = [ctx.enter_context(nc.sbuf_tensor(f"{tag}xt{i}", [128, D], F32)) for i in range(2)]
        lnb = ctx.enter_context(nc.sbuf_tensor(f"{tag}lnb", [128, D], BF16))
        dump = ctx.enter_context(nc.sbuf_tensor(f"{tag}dump", [128, D], BF16))
        st = ctx.enter_context(nc.sbuf_tensor(f"{tag}st", [128, 8], F32))
        ps = [ctx.enter_context(nc.psum_tensor(f"{tag}ps{i}", [128, 512], BF16)) for i in range(2)]
        xload, ssd, varr, rstdd, lnbr, tpd, evd, sser, vser = sems(9)

        with nc.Block() as block:
            @block.sync
            def _(s):
                for t in range(NT):
                    if t >= 2:
                        s.wait_ge(lnbr, t - 1)
                    s.dma_start(out=xt[t % 2][:], in_=x_dram_tiles[t]).then_inc(xload, 16)

            @block.scalar
            def _(s):
                ser = serializer(s, sser)
                for t in range(NT):
                    s.wait_ge(xload, 16 * (t + 1))
                    if t >= 1:
                        s.wait_ge(varr, t)
                    s.activation(dump[:], xt[t % 2][:], AF.Square,
                                 accum_out=st[:, 2:3]).then_inc(ssd, 1)
                    s.wait_ge(varr, t + 1)
                    ser(s.activation(st[:, 7:8], st[:, 5:6], AF.Ln, bias=epst[:]))
                    s.activation(st[:, 6:7], st[:, 7:8], AF.Exp, scale=-0.5).then_inc(rstdd, 1)

            @block.vector
            def _(v):
                ser = serializer(v, vser)
                for t in range(NT):
                    x = xt[t % 2]
                    v.wait_ge(xload, 16 * (t + 1))
                    ser(v.reduce_sum(st[:, 0:1], x[:], axis=AX.X))
                    ser(v.tensor_scalar(st[:, 1:2], st[:, 0:1], -1.0 / D, None, OP.mult))
                    v.wait_ge(ssd, t + 1)
                    v.tensor_scalar(st[:, 3:4], st[:, 2:3], 1.0 / D, None, OP.mult)
                    ser(v.tensor_tensor(st[:, 4:5], st[:, 1:2], st[:, 1:2], OP.mult))
                    v.tensor_tensor(st[:, 5:6], st[:, 3:4], st[:, 4:5],
                                    OP.subtract).then_inc(varr, 1)
                    v.wait_ge(rstdd, t + 1)
                    v.tensor_scalar(lnb[:], x[:], st[:, 1:2], st[:, 6:7],
                                    OP.add, OP.mult).then_inc(lnbr, 1)
                    for g in range(8):
                        v.wait_ge(tpd, 8 * t + g + 1)
                        v.tensor_copy(lnT[:, 4 * g:4 * (g + 1), 128 * t:128 * (t + 1)],
                                      ps[g % 2][:].rearrange("p (a b) -> p a b", a=4)
                                      ).then_inc(evd, 1)

            @block.tensor
            def _(te):
                for t in range(NT):
                    te.wait_ge(lnbr, t + 1)
                    for g in range(8):
                        gi = 8 * t + g
                        if gi >= 2:
                            te.wait_ge(evd, gi - 1)
                        inst = None
                        for jj in range(4):
                            j = 4 * g + jj
                            inst = te.transpose(ps[g % 2][:, 128 * jj:128 * (jj + 1)],
                                                lnb[:, 128 * j:128 * (j + 1)], ident[:])
                        inst.then_inc(tpd, 1)


def emit_proj_phase(nc, sems, tag, lhsT_sb, wt, NCH, CH, dst_of, clip,
                    extra_mm=None, evac_extra=None):
    with contextlib.ExitStack() as ctx:
        wbuf = [ctx.enter_context(nc.sbuf_tensor(f"{tag}wb{i}", [128, 32, CH], BF16)) for i in range(2)]
        ps = [ctx.enter_context(nc.psum_tensor(f"{tag}ps{i}", [128, CH], F32)) for i in range(2)]
        stage = [ctx.enter_context(nc.sbuf_tensor(f"{tag}stg{i}", [128, CH], BF16)) for i in range(2)]
        wload, mmd, evd, outd, pvser = sems(5)

        with nc.Block() as block:
            @block.gpsimd
            def _(g):
                for ch in range(NCH):
                    if ch >= 2:
                        g.wait_ge(mmd, NT * (ch - 1))
                    g.dma_start(out=wbuf[ch % 2][:, :16],
                                in_=wt.ap()[ch, :16].rearrange("a p c -> p a c")).then_inc(wload, 16)
                    g.dma_start(out=wbuf[ch % 2][:, 16:],
                                in_=wt.ap()[ch, 16:].rearrange("a p c -> p a c")).then_inc(wload, 16)

            @block.tensor
            def _(te):
                for ch in range(NCH):
                    te.wait_ge(wload, 32 * (ch + 1))
                    for t in range(NT):
                        idx = ch * NT + t
                        if idx >= 2:
                            te.wait_ge(evd, idx - 1)
                        pb = ps[idx % 2]
                        for d in range(32):
                            inst = te.matmul(pb[:], lhsT_sb[:, d, 128 * t:128 * (t + 1)],
                                             wbuf[ch % 2][:, d, :],
                                             start=(d == 0), stop=(d == 31 and extra_mm is None))
                        if extra_mm is not None:
                            inst = extra_mm(te, ch, t, pb)
                        inst.then_inc(mmd, 1)

            @block.vector
            def _(v):
                ser = serializer(v, pvser)
                for ch in range(NCH):
                    for t in range(NT):
                        idx = ch * NT + t
                        v.wait_ge(mmd, idx + 1)
                        if idx >= 2:
                            v.wait_ge(outd, 16 * (idx - 1))
                        if clip:
                            inst = v.tensor_scalar(stage[idx % 2][:], ps[idx % 2][:],
                                                   -CLIP, CLIP, OP.max, OP.min)
                        else:
                            inst = v.tensor_copy(stage[idx % 2][:], ps[idx % 2][:])
                        if evac_extra is not None:
                            ser(inst)
                            inst = evac_extra(v, ch, t, stage[idx % 2])
                        inst.then_inc(evd, 1)

            @block.sync
            def _(s):
                for ch in range(NCH):
                    for t in range(NT):
                        idx = ch * NT + t
                        s.wait_ge(evd, idx + 1)
                        s.dma_start(out=dst_of(ch, t), in_=stage[idx % 2][:]).then_inc(outd, 16)
                s.wait_ge(outd, 16 * NCH * NT)


def emit_p2b_pass(nc, sems, tag, src_nat, dstT, ident, do_ln, lnw_ap, lnb_ap,
                  kind, qn2_o=None, km2_o=None, epst=None):
    with contextlib.ExitStack() as ctx:
        qt = [ctx.enter_context(nc.sbuf_tensor(f"{tag}qt{i}", [128, D], BF16)) for i in range(2)]
        t1 = ctx.enter_context(nc.sbuf_tensor(f"{tag}t1", [128, D], F32))
        qln = ctx.enter_context(nc.sbuf_tensor(f"{tag}qln", [128, D], BF16))
        sqs = ctx.enter_context(nc.sbuf_tensor(f"{tag}sqs", [128, D], BF16))
        dump = ctx.enter_context(nc.sbuf_tensor(f"{tag}dump", [128, D], BF16))
        nrm = ctx.enter_context(nc.sbuf_tensor(f"{tag}nrm", [128, NT, H], F32))
        st = ctx.enter_context(nc.sbuf_tensor(f"{tag}st", [128, 8], F32))
        qT_sb = ctx.enter_context(nc.sbuf_tensor(f"{tag}qTsb", [128, 32, NT * 128], BF16))
        ps = [ctx.enter_context(nc.psum_tensor(f"{tag}ps{i}", [128, 512], BF16)) for i in range(2)]
        qload, ssd, varr, rstdd, lnr, sqr2, tpd, evd, find, sser, vser = sems(11)

        with nc.Block() as block:
            @block.sync
            def _(s):
                for t in range(NT):
                    if t >= 2:
                        s.wait_ge(lnr, t - 1)
                    s.dma_start(out=qt[t % 2][:], in_=src_nat.ap()[t]).then_inc(qload, 16)

            @block.scalar
            def _(s):
                ser = serializer(s, sser)
                for t in range(NT):
                    if do_ln:
                        s.wait_ge(qload, 16 * (t + 1))
                        if t >= 1:
                            s.wait_ge(varr, t)
                        s.activation(dump[:], qt[t % 2][:], AF.Square,
                                     accum_out=st[:, 2:3]).then_inc(ssd, 1)
                        s.wait_ge(varr, t + 1)
                        ser(s.activation(st[:, 7:8], st[:, 5:6], AF.Ln, bias=epst[:]))
                        s.activation(st[:, 6:7], st[:, 7:8], AF.Exp, scale=-0.5).then_inc(rstdd, 1)
                    s.wait_ge(lnr, t + 1)
                    s.activation(sqs[:], qln[:], AF.Square).then_inc(sqr2, 1)

            @block.vector
            def _(v):
                ser = serializer(v, vser)
                for t in range(NT):
                    q = qt[t % 2]
                    v.wait_ge(qload, 16 * (t + 1))
                    if do_ln:
                        ser(v.reduce_sum(st[:, 0:1], q[:], axis=AX.X))
                        ser(v.tensor_scalar(st[:, 1:2], st[:, 0:1], -1.0 / D, None, OP.mult))
                        v.wait_ge(ssd, t + 1)
                        v.tensor_scalar(st[:, 3:4], st[:, 2:3], 1.0 / D, None, OP.mult)
                        ser(v.tensor_tensor(st[:, 4:5], st[:, 1:2], st[:, 1:2], OP.mult))
                        v.tensor_tensor(st[:, 5:6], st[:, 3:4], st[:, 4:5],
                                        OP.subtract).then_inc(varr, 1)
                        v.wait_ge(rstdd, t + 1)
                        ser(v.tensor_scalar(t1[:], q[:], st[:, 1:2], st[:, 6:7], OP.add, OP.mult))
                        ser(v.tensor_tensor(t1[:], t1[:], lnw_ap, OP.mult))
                        v.tensor_tensor(qln[:], t1[:], lnb_ap, OP.add).then_inc(lnr, 1)
                    else:
                        v.tensor_copy(qln[:], q[:]).then_inc(lnr, 1)
                    v.wait_ge(sqr2, t + 1)
                    v.reduce_sum(nrm[:, t, :], sqs.ap().rearrange("p (h d) -> p h d", h=H),
                                 axis=AX.X)
                    for g in range(8):
                        v.wait_ge(tpd, 8 * t + g + 1)
                        v.tensor_copy(qT_sb[:, 4 * g:4 * (g + 1), 128 * t:128 * (t + 1)],
                                      ps[g % 2][:].rearrange("p (a b) -> p a b", a=4)
                                      ).then_inc(evd, 1)

            @block.tensor
            def _(te):
                for t in range(NT):
                    te.wait_ge(lnr, t + 1)
                    for g in range(8):
                        gi = 8 * t + g
                        if gi >= 2:
                            te.wait_ge(evd, gi - 1)
                        inst = None
                        for jj in range(4):
                            j = 4 * g + jj
                            inst = te.transpose(ps[g % 2][:, 128 * jj:128 * (jj + 1)],
                                                qln[:, 128 * j:128 * (j + 1)], ident[:])
                        inst.then_inc(tpd, 1)

        with nc.Block() as block:
            @block.sync
            def _(s):
                for gd in range(4):
                    s.dma_start(
                        out=dstT.ap()[8 * gd:8 * (gd + 1)].rearrange("a p i -> p a i"),
                        in_=qT_sb[:, 8 * gd:8 * (gd + 1), :]).then_inc(find, 16)
                s.wait_ge(find, 64)

            if kind == "q":
                @block.gpsimd
                def _(g):
                    g.dma_start(out=qn2_o.ap(),
                                in_=nrm.ap().rearrange("p a b -> p (a b)")).then_inc(find, 16)
                    g.wait_ge(find, 80)
            else:
                pmax = ctx.enter_context(nc.sbuf_tensor(f"{tag}pmax", [1, NT * H], F32))
                kmxt = ctx.enter_context(nc.sbuf_tensor(f"{tag}kmxt", [1, H], F32))

                @block.gpsimd
                def _(g):
                    g.tensor_reduce(pmax[:], nrm.ap().rearrange("p a b -> p (a b)"),
                                    axis=AX.C, op=OP.max).then_inc(find, 1)

                @block.vector
                def _(v):
                    v.wait_ge(find, 65)
                    v.reduce_max(kmxt[:], pmax[:].rearrange("p (a b) -> p b a", a=NT),
                                 axis=AX.X).then_inc(find, 1)

                @block.scalar
                def _(s):
                    s.wait_ge(find, 66)
                    s.dma_start(out=km2_o.ap(), in_=kmxt[:]).then_inc(find, 16)
                    s.wait_ge(find, 82)


def emit_ln_residual(nc, sems, tag, y_dram_tiles, x_dram_tiles, out_dram_tiles, epst=None):
    with contextlib.ExitStack() as ctx:
        yt = [ctx.enter_context(nc.sbuf_tensor(f"{tag}yt{i}", [128, D], BF16)) for i in range(2)]
        xt = [ctx.enter_context(nc.sbuf_tensor(f"{tag}xt{i}", [128, D], F32)) for i in range(2)]
        t1 = [ctx.enter_context(nc.sbuf_tensor(f"{tag}t1{i}", [128, D], F32)) for i in range(2)]
        dump = ctx.enter_context(nc.sbuf_tensor(f"{tag}dump", [128, D], BF16))
        st = ctx.enter_context(nc.sbuf_tensor(f"{tag}st", [128, 8], F32))
        yload, xload, ssd, varr, rstdd, resd, outd, sser, vser = sems(9)

        with nc.Block() as block:
            @block.sync
            def _(s):
                for t in range(NT):
                    if t >= 2:
                        s.wait_ge(resd, t - 1)
                    s.dma_start(out=yt[t % 2][:], in_=y_dram_tiles[t]).then_inc(yload, 16)
                    s.dma_start(out=xt[t % 2][:], in_=x_dram_tiles[t]).then_inc(xload, 16)

            @block.scalar
            def _(s):
                ser = serializer(s, sser)
                for t in range(NT):
                    s.wait_ge(yload, 16 * (t + 1))
                    if t >= 1:
                        s.wait_ge(varr, t)
                    s.activation(dump[:], yt[t % 2][:], AF.Square,
                                 accum_out=st[:, 2:3]).then_inc(ssd, 1)
                    s.wait_ge(varr, t + 1)
                    ser(s.activation(st[:, 7:8], st[:, 5:6], AF.Ln, bias=epst[:]))
                    s.activation(st[:, 6:7], st[:, 7:8], AF.Exp, scale=-0.5).then_inc(rstdd, 1)

            @block.vector
            def _(v):
                ser = serializer(v, vser)
                for t in range(NT):
                    y = yt[t % 2]
                    v.wait_ge(yload, 16 * (t + 1))
                    ser(v.reduce_sum(st[:, 0:1], y[:], axis=AX.X))
                    ser(v.tensor_scalar(st[:, 1:2], st[:, 0:1], -1.0 / D, None, OP.mult))
                    v.wait_ge(ssd, t + 1)
                    v.tensor_scalar(st[:, 3:4], st[:, 2:3], 1.0 / D, None, OP.mult)
                    ser(v.tensor_tensor(st[:, 4:5], st[:, 1:2], st[:, 1:2], OP.mult))
                    v.tensor_tensor(st[:, 5:6], st[:, 3:4], st[:, 4:5],
                                    OP.subtract).then_inc(varr, 1)
                    v.wait_ge(rstdd, t + 1)
                    if t >= 2:
                        v.wait_ge(outd, 16 * (t - 1))
                    ser(v.tensor_scalar(t1[t % 2][:], y[:], st[:, 1:2], st[:, 6:7],
                                        OP.add, OP.mult))
                    v.wait_ge(xload, 16 * (t + 1))
                    v.tensor_tensor(t1[t % 2][:], t1[t % 2][:], xt[t % 2][:],
                                    OP.add).then_inc(resd, 1)

            @block.gpsimd
            def _(g):
                for t in range(NT):
                    g.wait_ge(resd, t + 1)
                    g.dma_start(out=out_dram_tiles[t], in_=t1[t % 2][:]).then_inc(outd, 16)
                g.wait_ge(outd, 16 * NT)


def make_sem_factory(nc, ctx):
    state = {"i": 0}
    def sems(n):
        out = []
        for _ in range(n):
            out.append(ctx.enter_context(nc.semaphore(f"s{state['i']}")))
            state["i"] += 1
        return out
    return sems


def build_A(layer):
    nc = bass.Bass("TRN2", target_bir_lowering=False)
    CH = 512
    NCH = 24
    x = nc.dram_tensor("x", [NT, 128, D], F32, kind="ExternalInput")
    wt = nc.dram_tensor("wt", [NCH, 32, 128, CH], F32, kind="ExternalInput")
    ident_d = nc.dram_tensor("ident", [128, 128], BF16, kind="ExternalInput")
    lnw = {}
    if layer == 0:
        for nm in ("qw", "qb", "kw", "kb"):
            lnw[nm] = nc.dram_tensor(nm, [128, D], F32, kind="ExternalInput")
    qT_o = nc.dram_tensor("qT_o", [32, 128, NT * 128], BF16, kind="ExternalOutput")
    kT_o = nc.dram_tensor("kT_o", [32, 128, NT * 128], BF16, kind="ExternalOutput")
    v_o = nc.dram_tensor("v_o", [NT, 128, D], BF16, kind="ExternalOutput")
    qn2_o = nc.dram_tensor("qn2_o", [128, NT * H], F32, kind="ExternalOutput")
    km2_o = nc.dram_tensor("km2_o", [1, H], F32, kind="ExternalOutput")
    q_nat = nc.dram_tensor("q_nat", [NT, 128, D], BF16)
    k_nat = nc.dram_tensor("k_nat", [NT, 128, D], BF16)

    with contextlib.ExitStack() as ctx:
        sems = make_sem_factory(nc, ctx)
        ident = ctx.enter_context(nc.sbuf_tensor("identt", [128, 128], BF16))
        epst = ctx.enter_context(nc.sbuf_tensor("epst", [128, 1], F32))
        (ld0,) = sems(1)
        with nc.Block() as block:
            @block.sync
            def _(s):
                s.dma_start(out=ident[:], in_=ident_d.ap()).then_inc(ld0, 16)
                s.wait_ge(ld0, 16)

            @block.gpsimd
            def _(g):
                g.memset(epst[:], LN_EPS)

        with contextlib.ExitStack() as ctx2:
            lnT = ctx2.enter_context(nc.sbuf_tensor("lnT", [128, 32, NT * 128], BF16))
            emit_ln_lnT(nc, sems, "ln", [x.ap()[t] for t in range(NT)], lnT, ident, epst)

            def dst_of(ch, t):
                proj = ch // 8
                fc = (ch % 8) * CH
                dstn = (q_nat, k_nat, v_o)[proj]
                return dstn.ap()[t, :, fc:fc + CH]

            emit_proj_phase(nc, sems, "pj", lnT, wt, NCH, CH, dst_of, clip=(layer == 0))

        if layer == 0:
            wq = ctx.enter_context(nc.sbuf_tensor("wqv", [128, D], F32))
            bq = ctx.enter_context(nc.sbuf_tensor("bqv", [128, D], F32))
            wk = ctx.enter_context(nc.sbuf_tensor("wkv", [128, D], F32))
            bk = ctx.enter_context(nc.sbuf_tensor("bkv", [128, D], F32))
            (ld1,) = sems(1)
            with nc.Block() as block:
                @block.sync
                def _(s):
                    s.dma_start(out=wq[:], in_=lnw["qw"].ap()).then_inc(ld1, 16)
                    s.dma_start(out=bq[:], in_=lnw["qb"].ap()).then_inc(ld1, 16)
                    s.dma_start(out=wk[:], in_=lnw["kw"].ap()).then_inc(ld1, 16)
                    s.dma_start(out=bk[:], in_=lnw["kb"].ap()).then_inc(ld1, 16)
                    s.wait_ge(ld1, 64)
            emit_p2b_pass(nc, sems, "p2q", q_nat, qT_o, ident, True, wq[:], bq[:],
                          "q", qn2_o=qn2_o, epst=epst)
            emit_p2b_pass(nc, sems, "p2k", k_nat, kT_o, ident, True, wk[:], bk[:],
                          "k", km2_o=km2_o, epst=epst)
        else:
            emit_p2b_pass(nc, sems, "p2q", q_nat, qT_o, ident, False, None, None,
                          "q", qn2_o=qn2_o, epst=epst)
            emit_p2b_pass(nc, sems, "p2k", k_nat, kT_o, ident, False, None, None,
                          "k", km2_o=km2_o, epst=epst)
    return nc


def build_B(layer):
    nc = bass.Bass("TRN2", target_bir_lowering=False)
    CH = 256
    NCH = 16
    SBK = 4096
    x = nc.dram_tensor("x", [NT, 128, D], F32, kind="ExternalInput")
    qT = nc.dram_tensor("qT", [32, 128, NT * 128], BF16, kind="ExternalInput")
    KT = nc.dram_tensor("KT", [H, 128, SBK], BF16, kind="ExternalInput")
    Vf = nc.dram_tensor("Vf", [H, 128, 32 * 128], BF16, kind="ExternalInput")
    qn2 = nc.dram_tensor("qn2", [128, NT * H], F32, kind="ExternalInput")
    km2g = nc.dram_tensor("km2g", [4, H], F32, kind="ExternalInput")
    wo = nc.dram_tensor("wo", [NCH, 32, 128, CH], F32, kind="ExternalInput")
    ident_d = nc.dram_tensor("ident", [128, 128], BF16, kind="ExternalInput")
    slopes_d = iota_d = mask_d = qpos_d = None
    if layer == 0:
        iota_d = nc.dram_tensor("iota", [128, SBK], F32, kind="ExternalInput")
        mask_d = nc.dram_tensor("maskneg", [128, 512], F32, kind="ExternalInput")
        qpos_d = nc.dram_tensor("qpos", [128, NT], F32, kind="ExternalInput")
        slopes_d = nc.dram_tensor("slopes", [128, H], F32, kind="ExternalInput")
    x_new = nc.dram_tensor("x_new", [NT, 128, D], F32, kind="ExternalOutput")
    y_dram = nc.dram_tensor("y_dram", [NT, 128, D], BF16)

    nk_of = (lambda t: t + 1) if layer == 0 else (lambda t: 8)

    sc_at, tp_at, nrm_at, av_at, at_at = {}, {}, {}, {}, {}
    nsc = ntp = nht = 0
    for h in range(H):
        for t in range(NT):
            for j in range(nk_of(t)):
                nsc += 1
                sc_at[(h, t, j)] = nsc
            for g in range(nk_of(t)):
                ntp += 1
                tp_at[(h, t, g)] = ntp
            nht += 1
            nrm_at[(h, t)] = nht
            av_at[(h, t)] = nht
            at_at[(h, t)] = nht

    with contextlib.ExitStack() as ctx:
        sems = make_sem_factory(nc, ctx)
        ident = ctx.enter_context(nc.sbuf_tensor("identt", [128, 128], BF16))
        attnT = ctx.enter_context(nc.sbuf_tensor("attnT", [128, 32, NT * 128], BF16))
        nsh = ctx.enter_context(nc.sbuf_tensor("nsh", [128, NT, H], F32))
        epst = ctx.enter_context(nc.sbuf_tensor("epst", [128, 1], F32))

        with contextlib.ExitStack() as c1:
            qnb = c1.enter_context(nc.sbuf_tensor("qnb", [128, NT * H], F32))
            km4 = c1.enter_context(nc.sbuf_tensor("km4", [4, H], F32))
            km1 = c1.enter_context(nc.sbuf_tensor("km1", [1, H], F32))
            kmb = c1.enter_context(nc.sbuf_tensor("kmb", [128, H], F32))
            ones1 = c1.enter_context(nc.sbuf_tensor("ones1", [1, 128], F32))
            slp = c1.enter_context(nc.sbuf_tensor("slp", [128, H], F32))
            qps = c1.enter_context(nc.sbuf_tensor("qps", [128, NT], F32))
            sq = c1.enter_context(nc.sbuf_tensor("sqt", [128, H], F32))
            ps_bc = c1.enter_context(nc.psum_tensor("ps_bc", [128, H], F32))
            ld0, shd, sser1, vser1 = sems(4)

            nld = 16 * (5 if layer == 0 else 3)
            with nc.Block() as block:
                @block.sync
                def _(s):
                    s.dma_start(out=ident[:], in_=ident_d.ap()).then_inc(ld0, 16)
                    s.dma_start(out=qnb[:], in_=qn2.ap()).then_inc(ld0, 16)
                    s.dma_start(out=km4[:], in_=km2g.ap()).then_inc(ld0, 16)
                    if layer == 0:
                        s.dma_start(out=qps[:], in_=qpos_d.ap()).then_inc(ld0, 16)
                        s.dma_start(out=slp[:], in_=slopes_d.ap()).then_inc(ld0, 16)

                @block.gpsimd
                def _(g):
                    g.memset(ones1[:], 1.0)
                    g.memset(epst[:], LN_EPS)
                    g.wait_ge(ld0, nld)
                    g.tensor_reduce(km1[:], km4[:], axis=AX.C, op=OP.max).then_inc(shd, 1)

                @block.scalar
                def _(s):
                    ser = serializer(s, sser1)
                    s.wait_ge(shd, 1)
                    ser(s.activation(km1[:], km1[:], AF.Sqrt))
                    s.mul(km1[:], km1[:], -SCALE * 1.03).then_inc(shd, 1)
                    s.activation(qnb[:], qnb[:], AF.Sqrt).then_inc(shd, 1)

                @block.tensor
                def _(te):
                    te.wait_ge(shd, 2)
                    te.matmul(ps_bc[:], ones1[:], km1[:], start=True, stop=True).then_inc(shd, 1)

                @block.vector
                def _(v):
                    ser = serializer(v, vser1)
                    v.wait_ge(shd, 4)
                    ser(v.tensor_copy(kmb[:], ps_bc[:]))
                    for t in range(NT):
                        ser(v.tensor_tensor(nsh[:, t, :], qnb[:, t * H:(t + 1) * H], kmb[:],
                                            OP.mult))
                        if layer == 0:
                            ser(v.tensor_scalar(sq[:], slp[:], qps[:, t:t + 1], None, OP.mult))
                            ser(v.tensor_tensor(nsh[:, t, :], nsh[:, t, :], sq[:], OP.subtract))

        with contextlib.ExitStack() as c2:
            qTh = [c2.enter_context(nc.sbuf_tensor(f"qTh{i}", [128, NT * 128], BF16)) for i in range(2)]
            KTh = [c2.enter_context(nc.sbuf_tensor(f"KTh{i}", [128, SBK], BF16)) for i in range(2)]
            Vh = [c2.enter_context(nc.sbuf_tensor(f"Vh{i}", [128, 32, 128], BF16)) for i in range(2)]
            P = c2.enter_context(nc.sbuf_tensor("P", [128, SBK], BF16))
            PT = c2.enter_context(nc.sbuf_tensor("PT", [128, 32, 128], BF16))
            tf = [c2.enter_context(nc.sbuf_tensor(f"tf{i}", [128, 512], F32)) for i in range(2)]
            rs = c2.enter_context(nc.sbuf_tensor("rs", [128, 12], F32))
            alib = None
            iot = msk = None
            if layer == 0:
                alib = c2.enter_context(nc.sbuf_tensor("alib", [128, SBK], F32))
                iot = c2.enter_context(nc.sbuf_tensor("iot", [128, SBK], F32))
                msk = c2.enter_context(nc.sbuf_tensor("msk", [128, 512], F32))
            ps_sc = [c2.enter_context(nc.psum_tensor(f"ps_sc{i}", [128, 512], F32)) for i in range(2)]
            ps_pt = [c2.enter_context(nc.psum_tensor(f"ps_pt{i}", [128, 512], BF16)) for i in range(2)]
            ps_av = c2.enter_context(nc.psum_tensor("ps_av", [128, 128], F32))
            hload, scd, alid, expd, nrmd, tpd, ptd, avd, atd, alibd, mld, vser2 = sems(12)

            with nc.Block() as block:
                @block.sync
                def _(s):
                    if layer == 0:
                        s.dma_start(out=iot[:], in_=iota_d.ap()).then_inc(mld, 16)
                        s.dma_start(out=msk[:], in_=mask_d.ap()).then_inc(mld, 16)
                    for h in range(H):
                        if h >= 2:
                            s.wait_ge(atd, at_at[(h - 2, NT - 1)])
                        s.dma_start(out=qTh[h % 2][:], in_=qT.ap()[h]).then_inc(hload, 16)
                        s.dma_start(out=KTh[h % 2][:], in_=KT.ap()[h]).then_inc(hload, 16)
                        s.dma_start(out=Vh[h % 2][:].rearrange("p a b -> p (a b)"),
                                    in_=Vf.ap()[h]).then_inc(hload, 16)

                @block.vector
                def _(v):
                    ser = serializer(v, vser2)
                    for h in range(H):
                        if layer == 0:
                            if h == 0:
                                v.wait_ge(mld, 32)
                            ser(v.tensor_scalar(alib[:], iot[:], float(SLOPES[h] / SCALE),
                                                None, OP.mult))
                            v.sem_inc(alibd, 1)
                        for t in range(NT):
                            nk = nk_of(t)
                            if layer == 0:
                                for j in range(nk):
                                    i = sc_at[(h, t, j)]
                                    v.wait_ge(scd, i)
                                    if i >= 2:
                                        v.wait_ge(expd, i - 1)
                                    if j == t:
                                        ser(v.tensor_tensor(tf[i % 2][:], ps_sc[i % 2][:],
                                                            alib[:, 512 * j:512 * (j + 1)], OP.add))
                                        v.tensor_tensor(tf[i % 2][:], tf[i % 2][:], msk[:],
                                                        OP.add).then_inc(alid, 1)
                                    else:
                                        v.tensor_tensor(tf[i % 2][:], ps_sc[i % 2][:],
                                                        alib[:, 512 * j:512 * (j + 1)],
                                                        OP.add).then_inc(alid, 1)
                            v.wait_ge(expd, sc_at[(h, t, nk - 1)])
                            ser(v.reduce_sum(rs[:, 8:9], rs[:, 0:nk], axis=AX.X))
                            ser(v.reciprocal(rs[:, 9:10], rs[:, 8:9]))
                            v.tensor_scalar(P[:, :512 * nk], P[:, :512 * nk], rs[:, 9:10],
                                            None, OP.mult).then_inc(nrmd, 1)
                            for g in range(nk):
                                gi = tp_at[(h, t, g)]
                                v.wait_ge(tpd, gi)
                                v.tensor_copy(PT[:, 4 * g:4 * (g + 1), :],
                                              ps_pt[gi % 2][:].rearrange("p (a b) -> p a b", a=4)
                                              ).then_inc(ptd, 1)
                            v.wait_ge(avd, av_at[(h, t)])
                            v.tensor_copy(attnT[:, h, 128 * t:128 * (t + 1)],
                                          ps_av[:]).then_inc(atd, 1)

                @block.scalar
                def _(s):
                    for h in range(H):
                        for t in range(NT):
                            nk = nk_of(t)
                            for j in range(nk):
                                i = sc_at[(h, t, j)]
                                if (h, t) != (0, 0) and j == 0:
                                    prev = (h, t - 1) if t > 0 else (h - 1, NT - 1)
                                    s.wait_ge(tpd, tp_at[(prev[0], prev[1], nk_of(prev[1]) - 1)])
                                if layer == 0:
                                    s.wait_ge(alid, i)
                                    src_ap = tf[i % 2][:]
                                else:
                                    s.wait_ge(scd, i)
                                    src_ap = ps_sc[i % 2][:]
                                s.activation(P[:, 512 * j:512 * (j + 1)], src_ap, AF.Exp,
                                             bias=nsh[:, t, h:h + 1], scale=SCALE,
                                             accum_out=rs[:, j:j + 1]).then_inc(expd, 1)

                @block.tensor
                def _(te):
                    for h in range(H):
                        te.wait_ge(hload, 48 * (h + 1))
                        if layer == 0:
                            te.wait_ge(alibd, h + 1)
                        for t in range(NT):
                            nk = nk_of(t)
                            for j in range(nk):
                                i = sc_at[(h, t, j)]
                                if i >= 3:
                                    te.wait_ge(alid if layer == 0 else expd, i - 2)
                                te.matmul(ps_sc[i % 2][:],
                                          qTh[h % 2][:, 128 * t:128 * (t + 1)],
                                          KTh[h % 2][:, 512 * j:512 * (j + 1)],
                                          start=True, stop=True).then_inc(scd, 1)
                            te.wait_ge(nrmd, nrm_at[(h, t)])
                            for g in range(nk):
                                gi = tp_at[(h, t, g)]
                                if gi >= 3:
                                    te.wait_ge(ptd, gi - 2)
                                inst = None
                                for jj in range(4):
                                    kt = 4 * g + jj
                                    inst = te.transpose(ps_pt[gi % 2][:, 128 * jj:128 * (jj + 1)],
                                                        P[:, 128 * kt:128 * (kt + 1)], ident[:])
                                inst.then_inc(tpd, 1)
                            if at_at[(h, t)] >= 2:
                                te.wait_ge(atd, at_at[(h, t)] - 1)
                            for kt in range(4 * nk):
                                if kt % 4 == 0:
                                    te.wait_ge(ptd, tp_at[(h, t, kt // 4)])
                                inst = te.matmul(ps_av[:], Vh[h % 2][:, kt, :], PT[:, kt, :],
                                                 start=(kt == 0), stop=(kt == 4 * nk - 1))
                            inst.then_inc(avd, 1)

        def dst_of(ch, t):
            return y_dram.ap()[t, :, ch * CH:(ch + 1) * CH]

        emit_proj_phase(nc, sems, "op", attnT, wo, NCH, CH, dst_of, clip=False)

        emit_ln_residual(nc, sems, "lr", [y_dram.ap()[t] for t in range(NT)],
                         [x.ap()[t] for t in range(NT)],
                         [x_new.ap()[t] for t in range(NT)], epst=epst)
    return nc


def build_C():
    nc = bass.Bass("TRN2", target_bir_lowering=False)
    CH = 256
    NCH = 16
    x = nc.dram_tensor("x", [NT, 128, D], F32, kind="ExternalInput")
    vals = nc.dram_tensor("vals", [NCH, 32, 128, CH], I32, kind="ExternalInput")
    sc4 = nc.dram_tensor("sc4", [128, D], F32, kind="ExternalInput")
    sel4 = nc.dram_tensor("sel4", [128, 32 * 128], BF16, kind="ExternalInput")
    bias_b = nc.dram_tensor("bias_b", [128, D], F32, kind="ExternalInput")
    ident_d = nc.dram_tensor("ident", [128, 128], BF16, kind="ExternalInput")
    x_new = nc.dram_tensor("x_new", [NT, 128, D], F32, kind="ExternalOutput")
    y_dram = nc.dram_tensor("y_dram", [NT, 128, D], BF16)

    with contextlib.ExitStack() as ctx:
        sems = make_sem_factory(nc, ctx)
        ident = ctx.enter_context(nc.sbuf_tensor("identt", [128, 128], BF16))
        biasb = ctx.enter_context(nc.sbuf_tensor("biasb", [128, D], F32))
        sc4b = ctx.enter_context(nc.sbuf_tensor("sc4b", [128, D], BF16))
        sel4b = ctx.enter_context(nc.sbuf_tensor("sel4b", [128, 32, 128], BF16))
        epst = ctx.enter_context(nc.sbuf_tensor("epst", [128, 1], F32))
        epst2 = ctx.enter_context(nc.sbuf_tensor("epst2", [128, 1], F32))
        (ld0,) = sems(1)
        with nc.Block() as block:
            @block.gpsimd
            def _(g):
                g.memset(epst[:], LN_EPS)

            @block.sync
            def _(s):
                s.dma_start(out=ident[:], in_=ident_d.ap()).then_inc(ld0, 16)
                s.dma_start(out=biasb[:], in_=bias_b.ap()).then_inc(ld0, 16)
                s.wait_ge(ld0, 32)

            @block.gpsimd
            def _(g):
                g.memset(epst2[:], 0.0)
                g.dma_start(out=sc4b[:], in_=sc4.ap()).then_inc(ld0, 16)
                g.dma_start(out=sel4b[:].rearrange("p a b -> p (a b)"),
                            in_=sel4.ap()).then_inc(ld0, 16)
                g.wait_ge(ld0, 64)

        with contextlib.ExitStack() as c2:
            lnT = c2.enter_context(nc.sbuf_tensor("lnT", [128, 32, NT * 128], BF16))
            emit_ln_lnT(nc, sems, "ln", [x.ap()[t] for t in range(NT)], lnT, ident, epst)

            vf = [c2.enter_context(nc.sbuf_tensor(f"vf{i}", [128, 32, CH], BF16)) for i in range(2)]
            wde = [c2.enter_context(nc.sbuf_tensor(f"wde{i}", [128, 32, CH], BF16)) for i in range(2)]
            sctl = [c2.enter_context(nc.sbuf_tensor(f"sctl{i}", [128, 32, CH], BF16)) for i in range(2)]
            stage = [c2.enter_context(nc.sbuf_tensor(f"stg{i}", [128, CH], BF16)) for i in range(2)]
            ps = [c2.enter_context(nc.psum_tensor(f"cps{i}", [128, CH], F32)) for i in range(2)]
            ps_s = [c2.enter_context(nc.psum_tensor(f"csps{i}", [128, CH], F32)) for i in range(2)]
            vload, scmm, sevd, deqd, mmd, evd, outd, vser3 = sems(8)

            with nc.Block() as block:
                @block.gpsimd
                def _(g):
                    for ch in range(NCH):
                        if ch >= 2:
                            g.wait_ge(deqd, ch - 1)
                        g.dma_start(out=vf[ch % 2][:, :16],
                                    in_=vals.ap()[ch, :16].rearrange("a p c -> p a c")
                                    ).then_inc(vload, 16)
                        g.dma_start(out=vf[ch % 2][:, 16:],
                                    in_=vals.ap()[ch, 16:].rearrange("a p c -> p a c")
                                    ).then_inc(vload, 16)

                @block.tensor
                def _(te):
                    for ch in range(NCH):
                        for d in range(32):
                            sidx = ch * 32 + d
                            if sidx >= 2:
                                te.wait_ge(sevd, sidx - 1)
                            te.matmul(ps_s[sidx % 2][:], sel4b[:, d, :],
                                      sc4b[:, ch * CH:(ch + 1) * CH],
                                      start=True, stop=True).then_inc(scmm, 1)
                        te.wait_ge(deqd, ch + 1)
                        for t in range(NT):
                            idx = ch * NT + t
                            if idx >= 2:
                                te.wait_ge(evd, idx - 1)
                            pb = ps[idx % 2]
                            for d in range(32):
                                inst = te.matmul(pb[:], lnT[:, d, 128 * t:128 * (t + 1)],
                                                 wde[ch % 2][:, d, :],
                                                 start=(d == 0), stop=(d == 31))
                            inst.then_inc(mmd, 1)

                @block.vector
                def _(v):
                    ser = serializer(v, vser3)
                    for ch in range(NCH):
                        for d in range(32):
                            sidx = ch * 32 + d
                            v.wait_ge(scmm, sidx + 1)
                            inst = v.tensor_copy(sctl[ch % 2][:, d, :],
                                                 ps_s[sidx % 2][:])
                            if d == 31:
                                ser(inst)
                                v.sem_inc(sevd, 1)
                            else:
                                inst.then_inc(sevd, 1)
                        v.wait_ge(vload, 32 * (ch + 1))
                        if ch >= 2:
                            v.wait_ge(mmd, NT * (ch - 1))
                        ser(v.tensor_scalar(vf[ch % 2][:].rearrange("p a b -> p (a b)"),
                                            vf[ch % 2][:].rearrange("p a b -> p (a b)"),
                                            -128.0, None, OP.add))
                        v.tensor_tensor(wde[ch % 2][:].rearrange("p a b -> p (a b)"),
                                        vf[ch % 2][:].rearrange("p a b -> p (a b)"),
                                        sctl[ch % 2][:].rearrange("p a b -> p (a b)"),
                                        OP.mult).then_inc(deqd, 1)
                        for t in range(NT):
                            idx = ch * NT + t
                            v.wait_ge(mmd, idx + 1)
                            if idx >= 2:
                                v.wait_ge(outd, 16 * (idx - 1))
                            v.tensor_tensor(stage[idx % 2][:], ps[idx % 2][:],
                                            biasb[:, ch * CH:(ch + 1) * CH],
                                            OP.add).then_inc(evd, 1)

                @block.sync
                def _(s):
                    for ch in range(NCH):
                        for t in range(NT):
                            idx = ch * NT + t
                            s.wait_ge(evd, idx + 1)
                            s.dma_start(out=y_dram.ap()[t, :, ch * CH:(ch + 1) * CH],
                                        in_=stage[idx % 2][:]).then_inc(outd, 16)
                    s.wait_ge(outd, 16 * NCH * NT)

        emit_ln_residual(nc, sems, "lr", [y_dram.ap()[t] for t in range(NT)],
                         [x.ap()[t] for t in range(NT)],
                         [x_new.ap()[t] for t in range(NT)], epst=epst)
    return nc


class Chain:
    def __init__(self):
        self.items = []
    def add(self, eng, fn):
        self.items.append((eng, fn))
    def emit(self, nc, seq):
        items = self.items
        with nc.Block() as block:
            for engname in ("tensor", "vector", "scalar", "gpsimd", "sync"):
                ops = [(i, fn) for i, (e, fn) in enumerate(items) if e == engname]
                if not ops:
                    continue
                def mk(ops):
                    def _(eng):
                        for i, fn in ops:
                            if i > 0:
                                eng.wait_ge(seq, i)
                            inst = fn(eng)
                            inst.then_inc(seq, 1)
                    return _
                getattr(block, engname)(mk(ops))


def build_D():
    nc = bass.Bass("TRN2", target_bir_lowering=False)
    CH = 256
    NCH = 16
    x = nc.dram_tensor("x", [NT, 128, D], F32, kind="ExternalInput")
    wb_t = nc.dram_tensor("wb_t", [NCH, 32, 128, CH], F32, kind="ExternalInput")
    wgt = nc.dram_tensor("wgt", [128, 32, 16], F32, kind="ExternalInput")
    at_t = nc.dram_tensor("at_t", [128, 32, 128], F32, kind="ExternalInput")
    bf_t = nc.dram_tensor("bf_t", [128, D], F32, kind="ExternalInput")
    emat = nc.dram_tensor("emat", [8, 128], BF16, kind="ExternalInput")
    bias_b = nc.dram_tensor("bias_b", [128, D], F32, kind="ExternalInput")
    bthr = nc.dram_tensor("bthr", [128, 1], F32, kind="ExternalInput")
    ident_d = nc.dram_tensor("ident", [128, 128], BF16, kind="ExternalInput")
    out_o = nc.dram_tensor("out_o", [NT, 128, D], F32, kind="ExternalOutput")
    y_dram = nc.dram_tensor("y_dram", [NT, 128, D], BF16)

    with contextlib.ExitStack() as ctx:
        sems = make_sem_factory(nc, ctx)
        ident = ctx.enter_context(nc.sbuf_tensor("identt", [128, 128], BF16))
        biasb = ctx.enter_context(nc.sbuf_tensor("biasb", [128, D], F32))
        gT = ctx.enter_context(nc.sbuf_tensor("gT", [128, NT, 128], BF16))
        bfb = ctx.enter_context(nc.sbuf_tensor("bfb", [128, D], BF16))
        epst = ctx.enter_context(nc.sbuf_tensor("epst", [128, 1], F32))
        (ld0,) = sems(1)

        with contextlib.ExitStack() as c2:
            wgtb = c2.enter_context(nc.sbuf_tensor("wgtb", [128, 32, 16], BF16))
            atb = c2.enter_context(nc.sbuf_tensor("atb", [128, 32, 128], BF16))
            ematb = c2.enter_context(nc.sbuf_tensor("ematb", [8, 128], BF16))
            bthrb = c2.enter_context(nc.sbuf_tensor("bthrb", [128, 1], F32))

            with nc.Block() as block:
                @block.sync
                def _(s):
                    s.dma_start(out=ident[:], in_=ident_d.ap()).then_inc(ld0, 16)
                    s.dma_start(out=biasb[:], in_=bias_b.ap()).then_inc(ld0, 16)
                    s.dma_start(out=ematb[:], in_=emat.ap()).then_inc(ld0, 16)
                    s.dma_start(out=bthrb[:], in_=bthr.ap()).then_inc(ld0, 16)

                @block.gpsimd
                def _(g):
                    g.memset(epst[:], LN_EPS)
                    g.dma_start(out=wgtb[:].rearrange("p a b -> p (a b)"),
                                in_=wgt.ap().rearrange("p a b -> p (a b)")).then_inc(ld0, 16)
                    g.dma_start(out=atb[:].rearrange("p a b -> p (a b)"),
                                in_=at_t.ap().rearrange("p a b -> p (a b)")).then_inc(ld0, 16)
                    g.dma_start(out=bfb[:], in_=bf_t.ap()).then_inc(ld0, 16)
                    g.wait_ge(ld0, 112)

            lnT = c2.enter_context(nc.sbuf_tensor("lnT", [128, 32, NT * 128], BF16))
            emit_ln_lnT(nc, sems, "ln", [x.ap()[t] for t in range(NT)], lnT, ident, epst)

            gth = c2.enter_context(nc.sbuf_tensor("gth", [128, 16], F32))
            ge8 = c2.enter_context(nc.sbuf_tensor("ge8", [128, 8], F32))
            sst = c2.enter_context(nc.sbuf_tensor("sst", [128, 8], F32))
            hb = c2.enter_context(nc.sbuf_tensor("hb", [128, 128], BF16))
            hTb = c2.enter_context(nc.sbuf_tensor("hTb", [128, 128], BF16))
            w8 = c2.enter_context(nc.sbuf_tensor("w8", [128, 8], BF16))
            w8T = c2.enter_context(nc.sbuf_tensor("w8T", [8, 128], BF16))
            weT = c2.enter_context(nc.sbuf_tensor("weT", [128, 128], BF16))
            ps_g = c2.enter_context(nc.psum_tensor("ps_g", [128, 16], F32))
            ps_h = c2.enter_context(nc.psum_tensor("ps_h", [128, 128], F32))
            ps_t1 = c2.enter_context(nc.psum_tensor("ps_t1", [8, 128], BF16))
            ps_t2 = c2.enter_context(nc.psum_tensor("ps_t2", [128, 128], F32))
            ps_t3 = c2.enter_context(nc.psum_tensor("ps_t3", [128, 128], BF16))
            (seq,) = sems(1)

            chain = Chain()
            for t in range(NT):
                def mms(te, t=t):
                    for d in range(32):
                        te.matmul(ps_g[:, :9], lnT[:, d, 128 * t:128 * (t + 1)],
                                  wgtb[:, d, :9], start=(d == 0), stop=(d == 31))
                    for d in range(32):
                        inst = te.matmul(ps_h[:], lnT[:, d, 128 * t:128 * (t + 1)],
                                         atb[:, d, :], start=(d == 0), stop=(d == 31))
                    return inst
                chain.add("tensor", mms)
                chain.add("vector", lambda v: v.tensor_copy(gth[:, :9], ps_g[:, :9]))
                chain.add("vector", lambda v: v.reduce_max(sst[:, 0:1], gth[:, 0:8], axis=AX.X))
                chain.add("vector", lambda v: v.tensor_scalar(sst[:, 1:2], sst[:, 0:1],
                                                              -1.0, None, OP.mult))
                chain.add("scalar", lambda s: s.activation(ge8[:], gth[:, 0:8], AF.Exp,
                                                           bias=sst[:, 1:2],
                                                           accum_out=sst[:, 2:3]))
                chain.add("scalar", lambda s: s.activation(sst[:, 3:4], gth[:, 8:9],
                                                           AF.Sigmoid, bias=bthrb[:]))
                chain.add("vector", lambda v: v.reciprocal(sst[:, 4:5], sst[:, 2:3]))
                chain.add("vector", lambda v: v.tensor_scalar(ge8[:], ge8[:], sst[:, 4:5], None, OP.mult))
                chain.add("vector", lambda v: v.tensor_scalar(sst[:, 5:6], sst[:, 3:4], 0.125, None, OP.mult))
                chain.add("vector", lambda v: v.tensor_scalar(ge8[:], ge8[:], sst[:, 5:6], None, OP.subtract))
                chain.add("vector", lambda v: v.tensor_scalar(ge8[:], ge8[:], 0.0, None, OP.max))
                chain.add("vector", lambda v: v.reduce_sum(sst[:, 6:7], ge8[:], axis=AX.X))
                chain.add("vector", lambda v: v.tensor_scalar(sst[:, 6:7], sst[:, 6:7], 1e-30, None, OP.max))
                chain.add("vector", lambda v: v.reciprocal(sst[:, 7:8], sst[:, 6:7]))
                chain.add("vector", lambda v: v.tensor_scalar(w8[:], ge8[:], sst[:, 7:8], 2.0, OP.mult, OP.mult))
                chain.add("tensor", lambda te: te.transpose(ps_t1[:], w8[:], ident[:]))
                chain.add("vector", lambda v: v.tensor_copy(w8T[:], ps_t1[:]))
                chain.add("tensor", lambda te: te.matmul(ps_t2[:], ematb[:], w8T[:],
                                                         start=True, stop=True))
                chain.add("vector", lambda v: v.tensor_copy(weT[:], ps_t2[:]))
                chain.add("vector", lambda v: v.tensor_copy(hb[:], ps_h[:]))
                chain.add("tensor", lambda te: te.transpose(ps_t3[:], hb[:], ident[:]))
                chain.add("vector", lambda v: v.tensor_copy(hTb[:], ps_t3[:]))
                chain.add("vector",
                          lambda v, t=t: v.tensor_tensor(gT[:, t, :], hTb[:], weT[:], OP.mult))
            chain.emit(nc, seq)

            def extra_mm(te, ch, t, pb):
                return te.matmul(pb[:], gT[:, t, :], bfb[:, ch * CH:(ch + 1) * CH],
                                 start=False, stop=True)

            def evac_extra(v, ch, t, stg):
                return v.tensor_tensor(stg[:], stg[:], biasb[:, ch * CH:(ch + 1) * CH],
                                       OP.add)

            def dst_of(ch, t):
                return y_dram.ap()[t, :, ch * CH:(ch + 1) * CH]

            emit_proj_phase(nc, sems, "bp", lnT, wb_t, NCH, CH, dst_of, clip=False,
                            extra_mm=extra_mm, evac_extra=evac_extra)

        emit_ln_residual(nc, sems, "lr", [y_dram.ap()[t] for t in range(NT)],
                         [x.ap()[t] for t in range(NT)],
                         [out_o.ap()[t] for t in range(NT)], epst=epst)
    return nc



NPF32 = np.float32

def tile_w(wT, CH):
    NF = wT.shape[1]
    return np.ascontiguousarray(
        wT.reshape(32, 128, NF // CH, CH).transpose(2, 0, 1, 3))

def bcast(v):
    return np.ascontiguousarray(np.broadcast_to(np.asarray(v, NPF32), (128, len(v))))

def core_tiles(c):
    cc = c % 4
    return [4 * k + cc for k in range(8)]

def split_x(x_full):
    out = []
    for c in range(8):
        b = c // 4
        xt = x_full[b].reshape(32, 128, D)
        out.append(np.ascontiguousarray(xt[core_tiles(c)]))
    return out

def merge_x(parts):
    full = np.zeros((2, 32, 128, D), parts[0].dtype)
    for c in range(8):
        b = c // 4
        full[b, core_tiles(c)] = parts[c]
    return full.reshape(2, 4096, D)

def gather_kv(kT_cores, v_cores, km2_cores, b):
    ks = [kT_cores[4 * b + cc] for cc in range(4)]
    kt4 = np.stack(ks, axis=2)
    kt4 = kt4.reshape(H, 128, 4, 8, 128).transpose(0, 1, 3, 2, 4)
    KT_full = np.ascontiguousarray(kt4.reshape(H, 128, 4096))
    vs = np.stack([v_cores[4 * b + cc] for cc in range(4)], axis=0)
    vv = vs.reshape(4, 8, 128, 32, 128).transpose(3, 2, 1, 0, 4)
    V_full = np.ascontiguousarray(vv.reshape(H, 128, 4096))
    km2g = np.ascontiguousarray(
        np.stack([km2_cores[4 * b + cc][0] for cc in range(4)], axis=0))
    return KT_full, V_full, km2g

IDENT = np.eye(128, dtype=BF)
import ml_dtypes as _mld
_b = np.arange(128)[:, None]
_dtp = np.arange(32 * 128)[None, :]
SEL4 = np.ascontiguousarray((_b == 4 * (_dtp // 128) + (_dtp % 128) // 32).astype(_mld.bfloat16))
EMAT = np.ascontiguousarray((np.arange(128)[None, :] // 16 == np.arange(8)[:, None]).astype(BF))
IOTA = np.ascontiguousarray(np.broadcast_to(np.arange(4096, dtype=NPF32), (128, 4096)))
SLOPES_B = np.ascontiguousarray(np.broadcast_to(SLOPES.astype(NPF32), (128, H)))

def mask_qpos(c):
    cc = c % 4
    i = np.arange(128)
    j = np.arange(512)
    mask = np.where(j[None, :] <= 128 * cc + i[:, None], 0.0, -1.0e30).astype(NPF32)
    qpos = np.stack([(4 * t + cc) * 128 + i for t in range(8)], axis=1).astype(NPF32)
    return np.ascontiguousarray(mask), np.ascontiguousarray(qpos)

_BUILT = {}
def get_nc(name, fn):
    if name not in _BUILT:
        _BUILT[name] = fn()
    return _BUILT[name]


def launch_A(layer, x_cores, wt, lnw=None):
    nc = get_nc(f"A{layer}", lambda: build_A(layer))
    maps = []
    for c in range(8):
        m = {"x": x_cores[c], "wt": wt, "ident": IDENT}
        if layer == 0:
            m.update(qw=lnw[0], qb=lnw[1], kw=lnw[2], kb=lnw[3])
        maps.append(m)
    res = run_bass_kernel_spmd(nc, maps, core_ids=list(range(8)))
    r = res.results
    return ([r[c]["qT_o"] for c in range(8)], [r[c]["kT_o"] for c in range(8)],
            [r[c]["v_o"] for c in range(8)], [r[c]["qn2_o"] for c in range(8)],
            [r[c]["km2_o"] for c in range(8)])

def launch_B(layer, x_cores, qT, kT, v, qn2, km2, woT_tiled):
    nc = get_nc(f"B{layer}", lambda: build_B(layer))
    kv = [gather_kv(kT, v, km2, b) for b in range(2)]
    maps = []
    for c in range(8):
        b = c // 4
        KT_full, V_full, km2g = kv[b]
        m = {"x": x_cores[c], "qT": qT[c], "KT": KT_full, "Vf": V_full,
             "qn2": qn2[c], "km2g": km2g, "wo": woT_tiled, "ident": IDENT}
        if layer == 0:
            mask, qpos = mask_qpos(c)
            m.update(iota=IOTA, maskneg=mask, qpos=qpos, slopes=SLOPES_B)
        maps.append(m)
    res = run_bass_kernel_spmd(nc, maps, core_ids=list(range(8)))
    return [res.results[c]["x_new"] for c in range(8)]

def launch_C(x_cores, vals_t, sc4, bias_b):
    nc = get_nc("C", build_C)
    maps = [{"x": x_cores[c], "vals": vals_t, "sc4": sc4, "sel4": SEL4,
             "bias_b": bias_b, "ident": IDENT} for c in range(8)]
    res = run_bass_kernel_spmd(nc, maps, core_ids=list(range(8)))
    return [res.results[c]["x_new"] for c in range(8)]

def launch_D(x_cores, wb_t, wgt, at_t, bf_t, bias_b, bthr):
    nc = get_nc("D", build_D)
    maps = [{"x": x_cores[c], "wb_t": wb_t, "wgt": wgt, "at_t": at_t,
             "bf_t": bf_t, "emat": EMAT, "bias_b": bias_b, "bthr": bthr,
             "ident": IDENT} for c in range(8)]
    res = run_bass_kernel_spmd(nc, maps, core_ids=list(range(8)))
    return [res.results[c]["out_o"] for c in range(8)]


def run_all(inp):
    x_cores = split_x(np.asarray(inp["L0_attn_hidden_states"], NPF32))

    wqkvT = np.asarray(inp["L0_attn_w_qkv"], NPF32).T
    wt0 = tile_w(np.ascontiguousarray(wqkvT), 512)
    lnw0 = (bcast(inp["L0_attn_q_ln_w"]), bcast(inp["L0_attn_q_ln_b"]),
            bcast(inp["L0_attn_k_ln_w"]), bcast(inp["L0_attn_k_ln_b"]))
    qT, kT, v, qn2, km2 = launch_A(0, x_cores, wt0, lnw0)
    wo0 = tile_w(np.ascontiguousarray(np.asarray(inp["L0_attn_w_out"], NPF32).T), 256)
    x1 = launch_B(0, x_cores, qT, kT, v, qn2, km2, wo0)

    valsT = np.ascontiguousarray(np.asarray(inp["L0_ffn_q_vals"], np.int32)
                                 .reshape(4096, 4096).T)
    vals_t = np.ascontiguousarray(valsT.reshape(32, 128, 16, 256).transpose(2, 0, 1, 3))
    sc4 = np.ascontiguousarray(np.asarray(inp["L0_ffn_q_scales"], NPF32).T)
    bias0 = bcast(inp["L0_ffn_bias"])
    x2 = launch_C(x1, vals_t, sc4, bias0)

    w1 = np.concatenate([np.asarray(inp["L1_attn_w_q"], NPF32).T,
                         np.asarray(inp["L1_attn_w_k"], NPF32).T,
                         np.asarray(inp["L1_attn_w_v"], NPF32).T], axis=1)
    wt1 = tile_w(np.ascontiguousarray(w1), 512)
    qT, kT, v, qn2, km2 = launch_A(1, x2, wt1)
    wo1 = tile_w(np.ascontiguousarray(np.asarray(inp["L1_attn_w_o"], NPF32).T), 256)
    x3 = launch_B(1, x2, qT, kT, v, qn2, km2, wo1)

    wb_t = tile_w(np.ascontiguousarray(np.asarray(inp["L1_ffn_W_base"], NPF32).T), 256)
    G = np.concatenate([np.asarray(inp["L1_ffn_W_gate"], NPF32).T,
                        np.asarray(inp["L1_ffn_W_thr"], NPF32).T,
                        np.zeros((4096, 7), NPF32)], axis=1)
    wgt = np.ascontiguousarray(G.reshape(32, 128, 16).transpose(1, 0, 2))
    AT = np.asarray(inp["L1_ffn_A"], NPF32).reshape(128, 4096).T
    at_t = np.ascontiguousarray(AT.reshape(32, 128, 128).transpose(1, 0, 2))
    bf_t = np.ascontiguousarray(np.asarray(inp["L1_ffn_B"], NPF32)
                                .transpose(0, 2, 1).reshape(128, 4096))
    bias1 = bcast(inp["L1_ffn_b_base"])
    bthr = np.ascontiguousarray(
        np.broadcast_to(np.asarray(inp["L1_ffn_b_thr"], NPF32), (128, 1)))
    x4 = launch_D(x3, wb_t, wgt, at_t, bf_t, bias1, bthr)

    return merge_x(x4).astype(NPF32)


def kernel(**inputs):
    return run_all(inputs)
